# version 08
# baseline (speedup 1.0000x reference)
import numpy as np
from contextlib import ExitStack

import ml_dtypes
import concourse.bass as bass
import concourse.mybir as mybir
from concourse.bass import IndirectOffsetOnAxis
from concourse.bass_utils import run_bass_kernel_spmd
from concourse.library_config import all_libraries

F32 = mybir.dt.float32
BF16 = mybir.dt.bfloat16
I16 = mybir.dt.int16
I32 = mybir.dt.int32
U32 = mybir.dt.uint32
AF = mybir.ActivationFunctionType
ALU = mybir.AluOpType
AX = mybir.AxisListType
LIBS = {l.name: l for l in all_libraries}
BF = ml_dtypes.bfloat16

LN_EPS = 1e-5


def make_cfg(L=4096, n_cores=8, cap=640):
    B, D, NH, E, FF = 4, 1024, 16, 8, 2048
    Lc = L // n_cores
    T = B * Lc
    assert Lc % 256 == 0 and cap % 128 == 0
    return dict(
        B=B, L=L, D=D, NH=NH, Dh=D // NH, E=E, FF=FF,
        n_cores=n_cores, Lc=Lc, T=T, NT=T // 128, NC=D // 128,
        CAP=cap, NF=FF // 128, CAPT=cap // 128, NKV=4,
        NA=2 * T, NAC=2 * T // 16,
    )


class Sem:
    def __init__(self, C, name):
        self.C = C
        self.name = name
        self.h = C.ctx.enter_context(C.nc.semaphore(name))
        self.n = 0
        C.sems.append(self)

    def inc(self, inst, k=1):
        inst.then_inc(self.h, k)
        self.n += k
        return inst

    def mark(self, tag):
        key = (self.name, tag)
        if self.C.final:
            assert self.C.tags.get(key) == self.n, (key, self.C.tags.get(key), self.n)
        else:
            assert key not in self.C.tags, ("duplicate tag", key)
            self.C.tags[key] = self.n

    def wait(self, eng, tag):
        key = (self.name, tag)
        if self.C.final:
            assert key in self.C.tags, ("unknown tag", key)
        eng.wait_ge(self.h, self.C.tags.get(key, 0))


class Bld:
    def __init__(self, nc, ctx, tags, final, cfg):
        self.nc, self.ctx, self.tags, self.final, self.cfg = nc, ctx, tags, final, cfg
        self.sems = []


def build_nc(cfg, tags, final):
    nc = bass.Bass("TRN2", target_bir_lowering=False, debug=False)
    B, D, NH, E, FF = cfg["B"], cfg["D"], cfg["NH"], cfg["E"], cfg["FF"]
    Lc, T, NT, NC = cfg["Lc"], cfg["T"], cfg["NT"], cfg["NC"]
    CAP, NF, CAPT, NKV = cfg["CAP"], cfg["NF"], cfg["CAPT"], cfg["NKV"]
    NA, NAC = cfg["NA"], cfg["NAC"]
    NCORE = cfg["n_cores"]
    OQKV = 384 * NKV
    NCq = OQKV // 128
    Th = T // 2
    NTh = Th // 128
    NTpb = NT // B
    Lh = Lc // 2
    TBh = min(512, Th)
    NTBh = Th // TBh
    cap_blocks = []
    off = 0
    while off < CAP:
        w = min(512, CAP - off)
        cap_blocks.append((off, w))
        off += w
    NOB = max(1, D // 512)
    OBW = min(D, 512)
    NIDX = CAP // 16
    assert NIDX % 8 == 0

    dx = nc.dram_tensor("x", [T, D], F32, kind="ExternalInput").ap()
    dWq = nc.dram_tensor("WqT", [D, D], F32, kind="ExternalInput").ap()
    dWv = nc.dram_tensor("WvT", [D, D], F32, kind="ExternalInput").ap()
    dWo = nc.dram_tensor("WoT", [D, D], F32, kind="ExternalInput").ap()
    dWg = nc.dram_tensor("WgT", [D, D], F32, kind="ExternalInput").ap()
    ddecay = nc.dram_tensor("decay", [NH, 2, 1], F32, kind="ExternalInput").ap()
    dlng = nc.dram_tensor("ln_g", [D], F32, kind="ExternalInput").ap()
    dlnb = nc.dram_tensor("ln_b", [D], F32, kind="ExternalInput").ap()
    dg1 = nc.dram_tensor("ffn_g1", [D], F32, kind="ExternalInput").ap()
    db1 = nc.dram_tensor("ffn_b1", [D], F32, kind="ExternalInput").ap()
    dg2 = nc.dram_tensor("ffn_g2", [D], F32, kind="ExternalInput").ap()
    db2 = nc.dram_tensor("ffn_b2", [D], F32, kind="ExternalInput").ap()
    dWqkv = nc.dram_tensor("WqkvT", [D, OQKV], F32, kind="ExternalInput").ap()
    dWpr = nc.dram_tensor("WprojT", [D, D], F32, kind="ExternalInput").ap()
    dWr = nc.dram_tensor("Wr", [D, E], F32, kind="ExternalInput").ap()
    dW1 = nc.dram_tensor("moe_w1", [E, D, FF], F32, kind="ExternalInput").ap()
    dW2 = nc.dram_tensor("moe_w2", [E, FF, D], F32, kind="ExternalInput").ap()
    dout = nc.dram_tensor("out", [T + CAP, D], F32, kind="ExternalOutput").ap()

    dln2 = nc.dram_tensor("ln2_scratch", [T, D], F32).ap()
    dxn = nc.dram_tensor("xn_scratch", [NC, 128, T], BF16).ap()
    dstate = nc.dram_tensor("state_scratch", [NC, 128, T], BF16).ap()
    daout = nc.dram_tensor("aout_scratch", [T, D], BF16).ap()
    dwslot = nc.dram_tensor("w_slot", [16, NAC], F32).ap()
    dSin = nc.dram_tensor("S_in", [128, NC * B], F32).ap()
    dSout = nc.dram_tensor("S_out", [NCORE, 128, NC * B], F32).ap()
    dpid = nc.partition_id_tensor.ap()

    ident_b = nc.inline_tensor(np.eye(128, dtype=BF), name="ident_b").ap()
    ident_f = nc.inline_tensor(np.eye(128, dtype=np.float32), name="ident_f").ap()

    mseg = np.zeros((128, NC, NH), np.float32)
    e16 = np.zeros((16, NC, 128), np.float32)
    for cq in range(NC):
        g, j = cq // 2, cq % 2
        for p in range(128):
            h_ = 4 * g + 2 * j + (p // 64)
            mseg[p, cq, h_] = 0.125
            e16[h_, cq, p] = 1.0
    cMseg = nc.inline_tensor(mseg.astype(BF).reshape(128, NC * NH), name="Mseg").ap()
    cE16 = nc.inline_tensor(e16.astype(BF).reshape(16, NC * 128), name="E16").ap()

    sel = np.zeros((128, 8, 16), np.float32)
    selt = np.zeros((16, 8, 128), np.float32)
    for d in range(8):
        for q in range(16):
            sel[16 * d + q, d, q] = 1.0
            selt[q, d, 16 * d + q] = 1.0
    cSel = nc.inline_tensor(sel.reshape(128, 8 * 16), name="Sel").ap()
    cSelT = nc.inline_tensor(selt.reshape(16, 8 * 128), name="SelT").ap()

    aidp1 = (np.arange(16, dtype=np.float32)[:, None] * NAC
             + np.arange(NAC, dtype=np.float32)[None, :]) + 1.0
    cAIDp1 = nc.inline_tensor(aidp1, name="AIDp1").ap()

    colid = np.broadcast_to(np.arange(1, Lc + 1, dtype=np.float32),
                            (128, Lc)).copy()
    cColid = nc.inline_tensor(colid, name="colid").ap()

    lamsel = np.zeros((16, 2, 128), np.float32)
    lamsel[:, 0, :64] = 1.0
    lamsel[:, 1, 64:] = 1.0
    cLamsel = nc.inline_tensor(lamsel.reshape(16, 256), name="lamsel").ap()
    lammask = np.zeros((16, 2, NC), np.float32)
    for a in range(2):
        for c in range(NC):
            lammask[2 * c + a, a, c] = 1.0
    cLammask = nc.inline_tensor(lammask.reshape(16, 2 * NC), name="lammask").ap()

    o16 = (np.arange(NIDX, dtype=np.float32)[None, :] * 16
           + np.arange(16, dtype=np.float32)[:, None]).copy()
    cO16 = nc.inline_tensor(o16, name="O16").ap()
    tpcol = (T + 128.0 * np.arange(CAPT, dtype=np.float32)[None, :]
             + np.arange(128, dtype=np.float32)[:, None]).copy()
    cTp = nc.inline_tensor(tpcol, name="Tpcol").ap()

    ones_c1_f = nc.inline_tensor(np.ones((1, 128), np.float32), name="ones_c1f").ap()
    ones_col_b = nc.inline_tensor(np.ones((128, 1), BF), name="ones_colb").ap()
    ones_row_b = nc.inline_tensor(np.ones((1, TBh), BF), name="ones_rowb").ap()

    ctx = ExitStack()
    with ctx:
        C = Bld(nc, ctx, tags, final, cfg)

        def mksb(stack):
            def sb_(name, shape, dt=F32):
                return stack.enter_context(nc.sbuf_tensor(name, shape, dt))
            return sb_

        sbG = mksb(ctx)

        LdC = Sem(C, "LdC")
        LdX0, LdX1 = Sem(C, "LdX0"), Sem(C, "LdX1")
        LdWs = Sem(C, "LdWs")
        LdWt = Sem(C, "LdWt")
        LdW1 = Sem(C, "LdW1")
        LdW2 = Sem(C, "LdW2")
        LdS0, LdS1 = Sem(C, "LdS0"), Sem(C, "LdS1")
        LdA0, LdA1 = Sem(C, "LdA0"), Sem(C, "LdA1")
        LdH = Sem(C, "LdH")
        TE = Sem(C, "TE")
        VE = Sem(C, "VE")
        SE = Sem(C, "SE")
        GP = Sem(C, "GP")
        SSX = Sem(C, "SSX")
        SSS = Sem(C, "SSS")
        SSR = Sem(C, "SSR")
        SSC = Sem(C, "SSC")
        SSD = Sem(C, "SSD")
        SSW = Sem(C, "SSW")
        SSO = [Sem(C, "SSO0"), Sem(C, "SSO1")]
        SSA = [Sem(C, "SSA0"), Sem(C, "SSA1")]
        SSL = [Sem(C, "SSL0"), Sem(C, "SSL1")]
        SGT = Sem(C, "SGT")
        SGW = Sem(C, "SGW")
        SGS = Sem(C, "SGS")
        CCS = Sem(C, "CCS")

        PS = [ctx.enter_context(nc.psum_tensor(f"ps{i}", [128, 512], F32))
              for i in range(6)]
        PTA0 = ctx.enter_context(nc.psum_tensor("pta0", [128, 512], F32))
        PTA1 = ctx.enter_context(nc.psum_tensor("pta1", [128, 512], F32))

        def pta(k):
            return (PTA0, PTA1)[k % 2][:, 0:128]

        ident_fS = sbG("ident_fS", [128, 128])
        seltS = sbG("seltS", [16, 8 * 128])
        aidx = sbG("aidx", [128, E, NIDX], I16)
        idxu = sbG("idxu", [128, E, CAPT], mybir.dt.int32)
        gidx = sbG("gidx", [128, E, CAPT], mybir.dt.int32)
        sidx = sbG("sidx", [128, E, CAPT], mybir.dt.int32)
        nf_t = sbG("nf_t", [1, E], U32)

        ovAC = ExitStack()
        sb = mksb(ovAC)

        msegS = sb("msegS", [128, NC * NH], BF16)
        e16S = sb("e16S", [16, NC * 128], BF16)
        selS = sb("selS", [128, 8 * 16])
        aidp1S = sb("aidp1S", [16, NAC])
        colidS = sb("colidS", [128, Lc])
        lamselS = sb("lamselS", [16, 256])
        lammaskS = sb("lammaskS", [16, 2 * NC])
        lamrhs = sb("lamrhs", [16, 2 * NC])
        ones_c1S = sb("ones_c1S", [1, 128])
        ones_colS = sb("ones_colS", [128, 1], BF16)
        ones_rowS = sb("ones_rowS", [1, TBh], BF16)

        lam_all = sb("lam_all", [128, NC])
        lnlam = sb("lnlam", [128, NC])
        lng_all = sb("lng_all", [128, NC])
        lnb_all = sb("lnb_all", [128, NC])
        g1_all = sb("g1_all", [128, NC])
        b1_all = sb("b1_all", [128, NC], BF16)
        g2bc = sb("g2bc", [128, D], BF16)
        b2bc = sb("b2bc", [128, D], BF16)
        wr_b = sb("wr_b", [128, NC, E], BF16)
        cid_sb = sb("cid_sb", [1, 1], U32)
        cid_f = sb("cid_f", [1, 1])
        cid_bc = sb("cid_bc", [128, 1])
        lam_row = sb("lam_row", [NH, 1])
        bq_row = sb("bq_row", [1, OQKV], BF16)
        eps128 = sb("eps128", [128, 1])
        eps1 = sb("eps1", [1, 1])

        rows = sb("rows", [1, 4, TBh])
        assert 4 * TBh >= 2 * D
        rows_flat = rows.rearrange("one a b -> one (a b)")
        grow = rows_flat[:, 0:D]
        brow_ = rows_flat[:, D:2 * D]
        st_sum = sb("st_sum", [128, NT])
        st_sq = sb("st_sq", [128, NT])
        st_mu = sb("st_mu", [128, NT])
        st_var = sb("st_var", [128, NT])
        st_tmp = sb("st_tmp", [128, NT])
        st_tmp2 = sb("st_tmp2", [128, NT])
        st_r = sb("st_r", [128, NT])
        st_nmur = sb("st_nmur", [128, NT])
        st_rho = sb("st_rho", [128, NT])
        st2_sq = sb("st2_sq", [128, NT])
        st2_sum = sb("st2_sum", [128, NT])
        st2_r = sb("st2_r", [128, NT])
        st2_nmur = sb("st2_nmur", [128, NT])
        routepack = sb("routepack", [128, 4, NT])

        rho_row = sb("rho_row", [1, T])
        rho_t16 = sb("rho_t16", [NT, 128])
        rho_bc = sb("rho_bc", [128, Th], BF16)
        muy_bc = sb("muy_bc", [128, TBh], BF16)
        ry_bc = sb("ry_bc", [128, TBh], BF16)

        S_loc = sb("S_loc", [128, NC * B])
        S_all = sb("S_all", [128, NCORE, NC * B])
        carry = sb("carry", [128, NC * B])
        powk = sb("powk", [128, NC])
        ek_t = sb("ek_t", [128, 1])
        mask1 = sb("mask1", [128, 1])
        powchunk = sb("powchunk", [128, Lc], BF16)

        xbuf = sb("xbuf", [128, 2, D])
        sq_scr = sb("sq_scr", [128, D], BF16)
        ysq_scr = sb("ysq_scr", [128, TBh], BF16)
        stt_ld = sb("stt_ld", [128, 2, TBh], BF16)
        st_cor = sb("st_cor", [128, TBh], BF16)
        gat_ch = sb("gat_ch", [128, 2, TBh], BF16)
        aostg = sb("aostg", [128, 2, D], BF16)

        xnT = sb("xnT", [128, NC, Th], BF16)
        vT = sb("vT", [128, NC, Th], BF16)
        kvX = sb("kvX", [128, 2 * NKV, Th], BF16)
        wpool = sb("wpool", [128, 24, D], BF16)
        wt_wv = wpool[:, 0:8, :]
        wt_wq = wpool[:, 8:16, :]
        wt_wg = wpool[:, 16:24, :]
        wt_wo = wpool[:, 0:8, :]
        wt_wpr = wpool[:, 0:8, :]
        wqkv_t = wpool[:, 8:20, :].rearrange("p a b -> p (a b)") \
            .rearrange("p (c d) -> p c d", d=OQKV)

        prod_scr = sb("prod_scr", [128, 2, Lh], BF16)
        NPAIR = B * (B + 1) // 2
        attn_b = sb("attn_b", [16, NPAIR, Lh], BF16)
        smx = sb("smx", [16, Lh])
        srz = sb("srz", [16, Lh])
        av_tmp = sb("av_tmp", [128, Lh], BF16)

        xn_scr = sb("xn_scr", [128, D])
        h1b = sb("h1b", [128, 2, D])
        ln2b = sb("ln2b", [128, 2, D])
        l2T = sb("l2T", [128, 2, D], BF16)
        logit_s = sb("logit_s", [128, 8])
        mx8 = sb("mx8", [128, 8])
        ix8 = sb("ix8", [128, 8], U32)
        sgm = sb("sgm", [128, 1])

        E_IL = sb("E_IL", [16, NAC])
        W_IL = sb("W_IL", [16, NAC])
        candf = sb("candf", [16, NAC])
        candi = sb("candi", [16, NAC], I16)
        sgdst = sb("sgdst", [16, NAC], I16)
        tk_a = sb("tk_a", [16, NIDX], I32)
        tk_b = sb("tk_b", [16, NIDX], I32)
        tk_c = sb("tk_c", [16, NIDX], I32)
        o16S = sb("o16S", [16, NIDX])
        tpS = sb("tpS", [128, CAPT])
        nf_f = sb("nf_f", [1, E])
        nf16 = sb("nf16", [16, E])
        st3 = sb("st3", [16, 3 * NIDX])
        ug = sb("ug", [128, CAPT])
        uv = sb("uv", [128, CAPT])

        block = ctx.enter_context(nc.Block())

        def _cut_final():
            @block.gpsimd
            def _(e):
                for s_ in C.sems:
                    e.wait_ge(s_.h, s_.n)

        def a_tiles(h):
            return list(range(NTh * h, NTh * (h + 1)))

        def c_tiles(lh):
            return [t for t in range(NT) if (t % NTpb) // (NTpb // 2) == lh]

        ctile_order = c_tiles(0) + c_tiles(1)

        def lh_col(t, lh):
            b = t // NTpb
            lt = t % NTpb
            return b * Lh + (lt - (NTpb // 2) * lh) * 128

        def tb_batches(tb):
            lo = TBh * tb // Lc
            hi = (TBh * (tb + 1) - 1) // Lc
            return range(lo, hi + 1)

        @block.sync
        def _(e):
            for dst, src in (
                (ident_fS, ident_f), (msegS, cMseg),
                (e16S, cE16), (selS, cSel), (seltS, cSelT),
                (aidp1S, cAIDp1), (colidS, cColid), (lamselS, cLamsel),
                (lammaskS, cLammask), (o16S, cO16), (tpS, cTp),
                (ones_c1S, ones_c1_f),
                (ones_colS, ones_col_b), (ones_rowS, ones_row_b),
            ):
                LdC.inc(e.dma_start(out=dst[:], in_=src), 16)
            for dst, src in ((lng_all, dlng), (lnb_all, dlnb), (g1_all, dg1)):
                LdC.inc(e.dma_start(
                    out=dst[:], in_=src.rearrange("(c p) -> p c", p=128),
                    allow_slow_non_contiguous=True), 16)
            LdC.inc(e.dma_start(out=grow[:], in_=dg2[None, :]), 16)
            LdC.inc(e.dma_start(out=brow_[:], in_=db2[None, :]), 16)
            LdC.inc(e.dma_start(out=lam_row[:], in_=ddecay[:, 1, :],
                                allow_slow_non_contiguous=True), 16)
            LdC.inc(e.dma_start(out=cid_sb[:], in_=dpid[:, :]), 16)
            LdC.mark("prelude")

        for engname in ("tensor", "vector", "scalar", "gpsimd"):
            getattr(block, engname)(lambda e: LdC.wait(e, "prelude"))

        @block.gpsimd
        def _(e):
            LdWs.inc(e.dma_start(
                out=b1_all[:], in_=db1.rearrange("(c p) -> p c", p=128),
                allow_slow_non_contiguous=True), 16)
            LdWs.inc(e.dma_start(
                out=wr_b[:], in_=dWr.rearrange("(c p) e -> p c e", p=128)), 16)
            LdWs.mark("smallw")

        @block.vector
        def _(e):
            e.memset(eps128[:], LN_EPS)
            e.memset(eps1[:], LN_EPS)
            e.tensor_copy(cid_f[:], cid_sb[:])
            e.tensor_scalar_mul(lamrhs[:], lammaskS[:], lam_row[:])
            VE.inc(e.drain())
            VE.mark("cidf")

        @block.tensor
        def _(e):
            VE.wait(e, "cidf")
            for a in range(2):
                e.matmul(PS[4][:, 0:NC], lamselS[:, 128 * a:128 * (a + 1)],
                         lamrhs[:, NC * a:NC * (a + 1)],
                         start=(a == 0), stop=(a == 1))
            e.matmul(PS[4][:, NC:NC + 1], ones_c1S[:], cid_f[:],
                     start=True, stop=True)
            TE.inc(e.drain())
            TE.mark("lam_ps")

        @block.vector
        def _(e):
            TE.wait(e, "lam_ps")
            e.tensor_scalar_max(lam_all[:], PS[4][:, 0:NC], 1e-30)
            e.tensor_copy(cid_bc[:], PS[4][:, NC:NC + 1])
            VE.inc(e.drain())
            VE.mark("lam_sb")

        @block.scalar
        def _(e):
            VE.wait(e, "lam_sb")
            SE.inc(e.activation(out=lnlam[:], in_=lam_all[:], func=AF.Ln))
            SE.mark("lnlam")

        @block.tensor
        def _(e):
            for half in range(NOB):
                if half >= 1:
                    VE.wait(e, f"g2bc_ev{half - 1}")
                sl = slice(OBW * half, OBW * (half + 1))
                e.matmul(PS[2][:, 0:OBW], ones_c1S[:], grow[:, sl],
                         start=True, stop=True)
                e.matmul(PS[3][:, 0:OBW], ones_c1S[:], brow_[:, sl],
                         start=True, stop=True)
                TE.inc(e.drain())
                TE.mark(f"g2bc{half}")

        @block.vector
        def _(e):
            for half in range(NOB):
                TE.wait(e, f"g2bc{half}")
                sl = slice(OBW * half, OBW * (half + 1))
                e.tensor_copy(g2bc[:, sl], PS[2][:, 0:OBW])
                e.tensor_copy(b2bc[:, sl], PS[3][:, 0:OBW])
                VE.inc(e.drain())
                VE.mark(f"g2bc_ev{half}")
            VE.mark("g2bc_done")

        wt_prev = [None]

        def weight_transpose(wdram_t, O, dst, tag, scale=None, pre_wait=None):
            assert scale is None
            prev = wt_prev[0]
            wt_prev[0] = tag

            @block.gpsimd
            def _(e):
                if prev is not None:
                    LdWt.wait(e, f"wt_done_{prev}")
                if pre_wait is not None:
                    pre_wait(e)
                LdWt.inc(e.dma_start(
                    out=dst[:, :, :],
                    in_=wdram_t.rearrange("(c p) o -> p c o", p=128)), 16)
                LdWt.mark(f"wt_done_{tag}")

        if cfg.get("upto") == "P":
            _cut_final()
            ovAC.close()
            return nc

        weight_transpose(dWv, D, wt_wv, "wv",
                         pre_wait=lambda e: VE.wait(e, "g2bc_done"))

        if cfg.get("upto") == "WT":
            _cut_final()
            ovAC.close()
            return nc

        @block.sync
        def _(e):
            for h in range(2):
                for t in a_tiles(h):
                    if t >= 2:
                        TE.wait(e, f"xnT_tp_{(t - 2) * NC + NC - 1}")
                    LX = (LdX0, LdX1)[t % 2]
                    LX.inc(e.dma_start(out=xbuf[:, t % 2, :],
                                       in_=dx[128 * t:128 * (t + 1), :]), 16)
                    LX.mark(f"x_{t}")
                SE.wait(e, f"xnT_done_{h}")
                if cfg.get("no_big3d"):
                    for c3 in range(NC):
                        SSX.inc(e.dma_start(
                            out=dxn[c3, :, Th * h:Th * (h + 1)],
                            in_=xnT[:, c3, :]), 16)
                else:
                    SSX.inc(e.dma_start(
                        out=dxn[:, :, Th * h:Th * (h + 1)].rearrange("c p j -> p c j"),
                        in_=xnT[:, :, :]), 16)
                SSX.mark(f"xnst_{h}")

        @block.scalar
        def _(e):
            for h in range(2):
                for t in a_tiles(h):
                    tl = t - NTh * h
                    (LdX0, LdX1)[t % 2].wait(e, f"x_{t}")
                    e.activation(out=sq_scr[:], in_=xbuf[:, t % 2, :],
                                 func=AF.Square, accum_out=st_sq[:, t:t + 1])
                    SE.inc(e.drain())
                    SE.mark(f"sq_{t}")
                    VE.wait(e, f"var_{t}")
                    e.activation(out=st_tmp[:, t:t + 1], in_=st_var[:, t:t + 1],
                                 func=AF.Sqrt, bias=eps128[:])
                    SE.inc(e.drain())
                    SE.mark(f"sqrt_{t}")
                    for fc in range(NC):
                        k = t * NC + fc
                        TE.wait(e, f"xnT_tp_{k}")
                        e.copy(out=xnT[:, fc, 128 * tl:128 * (tl + 1)],
                               in_=pta(k))
                        SE.inc(e.drain())
                        SE.mark(f"xnT_ev_{k}")
                SE.mark(f"xnT_done_{h}")

        def xpass_half(h):
            @block.vector
            def _(e, h=h):
                for t in a_tiles(h):
                    (LdX0, LdX1)[t % 2].wait(e, f"x_{t}")
                    if t >= 2:
                        TE.wait(e, f"xnT_tp_{(t - 2) * NC + NC - 1}")
                    e.reduce_sum(st_sum[:, t:t + 1], xbuf[:, t % 2, :],
                                 axis=AX.X)
                    SE.wait(e, f"sq_{t}")
                    e.drain()
                    e.tensor_scalar_mul(st_mu[:, t:t + 1], st_sum[:, t:t + 1],
                                        1.0 / D)
                    e.drain()
                    e.scalar_tensor_tensor(st_tmp2[:, t:t + 1],
                                           st_mu[:, t:t + 1], -1.0,
                                           st_mu[:, t:t + 1],
                                           ALU.mult, ALU.mult)
                    e.tensor_scalar_mul(st_var[:, t:t + 1], st_sq[:, t:t + 1],
                                        1.0 / D)
                    e.drain()
                    e.tensor_add(st_var[:, t:t + 1], st_var[:, t:t + 1],
                                 st_tmp2[:, t:t + 1])
                    VE.inc(e.drain())
                    VE.mark(f"var_{t}")
                    SE.wait(e, f"sqrt_{t}")
                    e.reciprocal(st_r[:, t:t + 1], st_tmp[:, t:t + 1])
                    e.drain()
                    e.scalar_tensor_tensor(st_nmur[:, t:t + 1],
                                           st_mu[:, t:t + 1], -1.0,
                                           st_r[:, t:t + 1],
                                           ALU.mult, ALU.mult)
                    e.drain()
                    e.tensor_scalar(xbuf[:, t % 2, :], xbuf[:, t % 2, :],
                                    st_r[:, t:t + 1], st_nmur[:, t:t + 1],
                                    ALU.mult, ALU.add)
                    VE.inc(e.drain())
                    VE.mark(f"xn_{t}")

        for h in range(2):
            xpass_half(h)

            @block.tensor
            def _(e, h=h):
                if h >= 1:
                    SSX.wait(e, f"xnst_{h - 1}")
                for t in a_tiles(h):
                    VE.wait(e, f"xn_{t}")
                    for fc in range(NC):
                        k = t * NC + fc
                        if k >= 2:
                            SE.wait(e, f"xnT_ev_{k - 2}")
                        e.transpose(pta(k),
                                    xbuf[:, t % 2, 128 * fc:128 * (fc + 1)],
                                    ident_fS[:])
                        TE.inc(e.drain())
                        TE.mark(f"xnT_tp_{k}")

            @block.tensor
            def _(e, h=h):
                LdWt.wait(e, "wt_done_wv")
                SE.wait(e, f"xnT_done_{h}")
                if h >= 1:
                    SSS.wait(e, "stst_0")
                for oc in range(NC):
                    for tb in range(NTBh):
                        k = oc * NTBh + tb
                        bank = PS[k % 2]
                        if k >= 2:
                            VE.wait(e, f"pj_ev_wv_{h}_{k - 2}")
                        elif h >= 1:
                            VE.wait(e, f"pj_ev_wv_0_{NC * NTBh - 2 + k}")
                        for fc in range(NC):
                            e.matmul(bank[:, 0:TBh],
                                     wt_wv[:, fc, 128 * oc:128 * (oc + 1)],
                                     xnT[:, fc, TBh * tb:TBh * (tb + 1)],
                                     start=(fc == 0), stop=(fc == NC - 1))
                        TE.inc(e.drain())
                        TE.mark(f"pj_wv_{h}_{k}")

            @block.vector
            def _(e, h=h):
                for oc in range(NC):
                    for tb in range(NTBh):
                        k = oc * NTBh + tb
                        bank = PS[k % 2]
                        TE.wait(e, f"pj_wv_{h}_{k}")
                        e.tensor_copy(vT[:, oc, TBh * tb:TBh * (tb + 1)],
                                      bank[:, 0:TBh])
                        VE.inc(e.drain())
                        VE.mark(f"pj_ev_wv_{h}_{k}")
                VE.mark(f"pj_done_wv_{h}")

            @block.vector
            def _(e, h=h):
                SE.wait(e, "lnlam")
                for oc in range(NC):
                    e.memset(xn_scr[:, 0:Lc], 1.0)
                    e.drain()
                    e.tensor_scalar_mul(xn_scr[:, 0:Lc], xn_scr[:, 0:Lc],
                                        lam_all[:, oc:oc + 1])
                    e.drain()
                    for bl in range(2):
                        b = 2 * h + bl
                        if cfg.get("no_scan"):
                            e.tensor_mul(
                                vT[:, oc, Lc * bl:Lc * (bl + 1)],
                                xn_scr[:, 0:Lc],
                                vT[:, oc, Lc * bl:Lc * (bl + 1)])
                        else:
                            e.tensor_tensor_scan(
                                vT[:, oc, Lc * bl:Lc * (bl + 1)],
                                xn_scr[:, 0:Lc],
                                vT[:, oc, Lc * bl:Lc * (bl + 1)],
                                0.0, ALU.mult, ALU.add)
                        e.drain()
                        e.tensor_copy(S_loc[:, oc * B + b:oc * B + b + 1],
                                      vT[:, oc, Lc * (bl + 1) - 1:Lc * (bl + 1)])
                        e.drain()
                VE.inc(e.drain())
                VE.mark(f"scan_done_{h}")

            @block.sync
            def _(e, h=h):
                VE.wait(e, f"scan_done_{h}")
                if cfg.get("no_big3d"):
                    for c3 in range(NC):
                        SSS.inc(e.dma_start(
                            out=dstate[c3, :, Th * h:Th * (h + 1)],
                            in_=vT[:, c3, :]), 16)
                else:
                    SSS.inc(e.dma_start(
                        out=dstate[:, :, Th * h:Th * (h + 1)].rearrange(
                            "c p j -> p c j"),
                        in_=vT[:, :, :]), 16)
                SSS.mark(f"stst_{h}")

        if cfg.get("upto") == "XP":
            _cut_final()
            ovAC.close()
            return nc

        @block.vector
        def _(e):
            VE.wait(e, f"xn_{NT - 1}")
            e.tensor_scalar_add(st_tmp2[:], st_var[:], LN_EPS)
            e.drain()
            e.reciprocal(st_tmp2[:], st_tmp2[:])
            e.drain()
            e.tensor_mul(st_rho[:], st_var[:], st_tmp2[:])
            VE.inc(e.drain())
            VE.mark("vn_rho")
            SE.wait(e, "rho_sqrt")
            e.reciprocal(st_rho[:], st_tmp2[:])
            VE.inc(e.drain())
            VE.mark("rho_done")

        @block.scalar
        def _(e):
            VE.wait(e, "vn_rho")
            e.activation(out=st_tmp2[:], in_=st_rho[:], func=AF.Sqrt,
                         bias=eps128[:])
            SE.inc(e.drain())
            SE.mark("rho_sqrt")

        @block.tensor
        def _(e):
            VE.wait(e, "rho_done")
            VE.wait(e, "g2bc_done")
            e.transpose(PS[5][0:NT, 0:128], st_rho[:], ident_fS[:])
            TE.inc(e.drain())
            TE.mark("rho_tp")

        @block.vector
        def _(e):
            TE.wait(e, "rho_tp")
            e.tensor_copy(rho_t16[:], PS[5][0:NT, 0:128])
            VE.inc(e.drain())
            VE.mark("rho_t16")

        @block.sync
        def _(e):
            VE.wait(e, "rho_t16")
            SSR.inc(e.dma_start(
                out=rho_row.rearrange("one (c p) -> one c p", p=128),
                in_=rho_t16[:]), 16)
            SSR.mark("rho_row")

        if cfg.get("upto") == "RHO":
            _cut_final()
            ovAC.close()
            return nc

        @block.sync
        def _(e):
            VE.wait(e, "scan_done_1")
            SSC.inc(e.dma_start(out=dSin, in_=S_loc[:]), 16)
            SSC.mark("S_stored")

        @block.gpsimd
        def _(e):
            SSC.wait(e, "S_stored")
            CCS.inc(e.collective_compute(
                "AllGather", ALU.bypass,
                replica_groups=[list(range(NCORE))],
                ins=[dSin.opt()], outs=[dSout.opt()]), 1)
            CCS.mark("ag_done")

        @block.sync
        def _(e):
            CCS.wait(e, "ag_done")
            SSC.inc(e.dma_start(
                out=S_all[:],
                in_=dSout.rearrange("r p n -> p r n")), 16)
            SSC.mark("S_all")

        @block.vector
        def _(e):
            SSC.wait(e, "S_all")
            e.memset(carry[:], 0.0)
            e.drain()
            for k in range(NCORE):
                e.tensor_scalar(ek_t[:], cid_bc[:], float(-(k + 1)), float(Lc),
                                ALU.add, ALU.mult)
                e.drain()
                for c in range(NC):
                    e.tensor_scalar_mul(powk[:, c:c + 1], lnlam[:, c:c + 1],
                                        ek_t[:])
                e.drain()
                e.tensor_scalar_min(powk[:], powk[:], 0.0)
                VE.inc(e.drain())
                VE.mark(f"powk_{k}")
                SE.wait(e, f"powk_e_{k}")
                e.tensor_single_scalar(mask1[:], ek_t[:], 0.0, ALU.is_ge)
                e.drain()
                for c in range(NC):
                    e.tensor_scalar_mul(powk[:, c:c + 1], powk[:, c:c + 1],
                                        mask1[:])
                e.drain()
                for c in range(NC):
                    for b in range(B):
                        col = c * B + b
                        e.scalar_tensor_tensor(
                            carry[:, col:col + 1], S_all[:, k, col:col + 1],
                            powk[:, c:c + 1], carry[:, col:col + 1],
                            ALU.mult, ALU.add)
                VE.inc(e.drain())
                VE.mark(f"carry_k_{k}")
            VE.mark("carry_done")

        @block.scalar
        def _(e):
            for k in range(NCORE):
                VE.wait(e, f"powk_{k}")
                e.activation(out=powk[:], in_=powk[:], func=AF.Exp)
                SE.inc(e.drain())
                SE.mark(f"powk_e_{k}")

        if cfg.get("upto") == "A1":
            _cut_final()
            ovAC.close()
            return nc

        weight_transpose(dWq, D, wt_wq, "wq")
        weight_transpose(dWg, D, wt_wg, "wg")
        weight_transpose(dWo, D, wt_wo, "wo",
                         pre_wait=lambda e: VE.wait(e, "pj_done_wv_1"))

        part2_halves = [1, 0]
        for hi, h in enumerate(part2_halves):
            if h == 0:
                @block.sync
                def _(e, h=h):
                    TE.wait(e, f"gate_{1}_{NC * NTBh - 1}")
                    SSX.wait(e, "xnst_1")
                    LdH.inc(e.dma_start(
                        out=xnT[:, :, :],
                        in_=dxn[:, :, 0:Th].rearrange("c p j -> p c j")), 16)
                    LdH.mark("xnre_0")

            @block.sync
            def _(e, h=h, hi=hi):
                SSS.wait(e, "stst_1")
                for oc in range(NC):
                    for tb in range(NTBh):
                        k = oc * NTBh + tb
                        LS = (LdS0, LdS1)[k % 2]
                        if k >= 2:
                            VE.wait(e, f"y_ev_{h}_{k - 2}")
                        elif hi >= 1:
                            VE.wait(e, f"y_ev_{part2_halves[0]}_"
                                       f"{NC * NTBh - 2 + k}")
                        LS.inc(e.dma_start(
                            out=stt_ld[:, k % 2, :],
                            in_=dstate[oc, :, Th * h + TBh * tb:
                                       Th * h + TBh * (tb + 1)]), 16)
                        LS.mark(f"st_{h}_{k}")

            @block.tensor
            def _(e, h=h, hi=hi):
                LdWt.wait(e, "wt_done_wq")
                if h == 0:
                    LdH.wait(e, "xnre_0")
                else:
                    SE.wait(e, f"xnT_done_{h}")
                if hi == 0:
                    VE.wait(e, "pj_done_wv_1")
                else:
                    VE.wait(e, f"wo_done_{part2_halves[0]}")
                for oc in range(NC):
                    for tb in range(NTBh):
                        k = oc * NTBh + tb
                        bank = PS[k % 2]
                        if k >= 2:
                            VE.wait(e, f"y_ev_{h}_{k - 2}")
                        for fc in range(NC):
                            e.matmul(bank[:, 0:TBh],
                                     wt_wq[:, fc, 128 * oc:128 * (oc + 1)],
                                     xnT[:, fc, TBh * tb:TBh * (tb + 1)],
                                     start=(fc == 0), stop=(fc == NC - 1))
                        TE.inc(e.drain())
                        TE.mark(f"pj_wq_{h}_{k}")

            @block.vector
            def _(e, h=h):
                VE.wait(e, "carry_done")
                for oc in range(NC):
                    e.tensor_scalar_mul(xn_scr[:, 0:Lc], colidS[:],
                                        lnlam[:, oc:oc + 1])
                    VE.inc(e.drain())
                    VE.mark(f"pw_{h}_{oc}")
                    SE.wait(e, f"pw_e_{h}_{oc}")
                    for tb in range(NTBh):
                        k = oc * NTBh + tb
                        (LdS0, LdS1)[k % 2].wait(e, f"st_{h}_{k}")
                        for bl in tb_batches(tb):
                            b = 2 * h + bl
                            c0 = Lc * bl - TBh * tb
                            w = Lc
                            e.scalar_tensor_tensor(
                                st_cor[:, c0:c0 + w], powchunk[:, 0:w],
                                carry[:, oc * B + b:oc * B + b + 1],
                                stt_ld[:, k % 2, c0:c0 + w],
                                ALU.mult, ALU.add)
                        e.drain()
                        TE.wait(e, f"pj_wq_{h}_{k}")
                        e.tensor_mul(vT[:, oc, TBh * tb:TBh * (tb + 1)],
                                     PS[k % 2][:, 0:TBh], st_cor[:, :])
                        VE.inc(e.drain())
                        VE.mark(f"y_ev_{h}_{k}")
                VE.mark(f"y_done_{h}")

            @block.scalar
            def _(e, h=h):
                for oc in range(NC):
                    VE.wait(e, f"pw_{h}_{oc}")
                    e.activation(out=powchunk[:, :], in_=xn_scr[:, 0:Lc],
                                 func=AF.Exp)
                    SE.inc(e.drain())
                    SE.mark(f"pw_e_{h}_{oc}")

            @block.vector
            def _(e, h=h):
                VE.wait(e, f"y_done_{h}")
                for tb in range(NTBh):
                    for oc in range(NC):
                        e.tensor_mul(ysq_scr[:, :],
                                     vT[:, oc, TBh * tb:TBh * (tb + 1)],
                                     vT[:, oc, TBh * tb:TBh * (tb + 1)])
                        VE.inc(e.drain())
                        VE.mark(f"ysq_{h}_{tb}_{oc}")
                        TE.wait(e, f"ystat_{h}_{tb}_{oc}")
                    e.tensor_copy(rows[:, 0, :], PS[4][0:1, 0:TBh])
                    e.tensor_copy(rows[:, 1, :], PS[5][0:1, 0:TBh])
                    VE.inc(e.drain())
                    VE.mark(f"ystat_ev_{h}_{tb}")
                    e.tensor_scalar_mul(rows[:, 2, :], rows[:, 0, :], 1.0 / D)
                    e.drain()
                    e.scalar_tensor_tensor(rows[:, 3, :], rows[:, 2, :], -1.0,
                                           rows[:, 2, :], ALU.mult, ALU.mult)
                    e.tensor_scalar_mul(rows[:, 1, :], rows[:, 1, :], 1.0 / D)
                    e.drain()
                    e.tensor_add(rows[:, 3, :], rows[:, 3, :], rows[:, 1, :])
                    VE.inc(e.drain())
                    VE.mark(f"yvar_{h}_{tb}")
                    SE.wait(e, f"ysd_{h}_{tb}")
                    e.reciprocal(rows[:, 3, :], rows[:, 3, :])
                    e.drain()
                    e.scalar_tensor_tensor(rows[:, 2, :], rows[:, 2, :], -1.0,
                                           rows[:, 3, :], ALU.mult, ALU.mult)
                    VE.inc(e.drain())
                    VE.mark(f"yrows_{h}_{tb}")
                    TE.wait(e, f"ybc_{h}_{tb}")
                    e.tensor_copy(muy_bc[:, :], PS[2][:, 0:TBh])
                    e.tensor_copy(ry_bc[:, :], PS[3][:, 0:TBh])
                    VE.inc(e.drain())
                    VE.mark(f"ybc_ev_{h}_{tb}")
                    for oc in range(NC):
                        n = tb * NC + oc
                        sl = slice(TBh * tb, TBh * (tb + 1))
                        e.tensor_mul(vT[:, oc, sl], vT[:, oc, sl], ry_bc[:, :])
                        e.drain()
                        e.tensor_add(vT[:, oc, sl], vT[:, oc, sl], muy_bc[:, :])
                        e.drain()
                        e.tensor_scalar(vT[:, oc, sl], vT[:, oc, sl],
                                        lng_all[:, oc:oc + 1],
                                        lnb_all[:, oc:oc + 1],
                                        ALU.mult, ALU.add)
                        e.drain()
                        SE.wait(e, f"gate_ev_{h}_{n}")
                        e.tensor_mul(vT[:, oc, sl], vT[:, oc, sl],
                                     gat_ch[:, n % 2, :])
                        VE.inc(e.drain())
                        VE.mark(f"yg_{h}_{n}")
                VE.mark(f"yg_done_{h}")

            @block.tensor
            def _(e, h=h, hi=hi):
                for tb in range(NTBh):
                    if tb >= 1:
                        VE.wait(e, f"ystat_ev_{h}_{tb - 1}")
                    elif hi >= 1:
                        VE.wait(e, f"ystat_ev_{part2_halves[0]}_{NTBh - 1}")
                    for oc in range(NC):
                        VE.wait(e, f"ysq_{h}_{tb}_{oc}")
                        e.matmul(PS[4][0:1, 0:TBh], ones_colS[:],
                                 vT[:, oc, TBh * tb:TBh * (tb + 1)],
                                 start=(oc == 0), stop=(oc == NC - 1))
                        e.matmul(PS[5][0:1, 0:TBh], ones_colS[:], ysq_scr[:, :],
                                 start=(oc == 0), stop=(oc == NC - 1))
                        TE.inc(e.drain())
                        TE.mark(f"ystat_{h}_{tb}_{oc}")
                    VE.wait(e, f"yrows_{h}_{tb}")
                    if tb >= 1:
                        VE.wait(e, f"ybc_ev_{h}_{tb - 1}")
                    elif hi >= 1:
                        VE.wait(e, f"ybc_ev_{part2_halves[0]}_{NTBh - 1}")
                    e.matmul(PS[2][:, 0:TBh], ones_c1S[:], rows[:, 2, :],
                             start=True, stop=True)
                    e.matmul(PS[3][:, 0:TBh], ones_c1S[:], rows[:, 3, :],
                             start=True, stop=True)
                    TE.inc(e.drain())
                    TE.mark(f"ybc_{h}_{tb}")
                    if tb == 0:
                        LdWt.wait(e, "wt_done_wg")
                        VE.wait(e, f"y_done_{h}")
                    for oc in range(NC):
                        n = tb * NC + oc
                        bank = PS[n % 2]
                        if n >= 2:
                            SE.wait(e, f"gate_ev_{h}_{n - 2}")
                        for fc in range(NC):
                            e.matmul(bank[:, 0:TBh],
                                     wt_wg[:, fc, 128 * oc:128 * (oc + 1)],
                                     xnT[:, fc, TBh * tb:TBh * (tb + 1)],
                                     start=(fc == 0), stop=(fc == NC - 1))
                        TE.inc(e.drain())
                        TE.mark(f"gate_{h}_{n}")

            @block.scalar
            def _(e, h=h):
                for tb in range(NTBh):
                    VE.wait(e, f"yvar_{h}_{tb}")
                    e.activation(out=rows[:, 3, :], in_=rows[:, 3, :],
                                 func=AF.Sqrt, bias=eps1[:])
                    SE.inc(e.drain())
                    SE.mark(f"ysd_{h}_{tb}")
                    for oc in range(NC):
                        n = tb * NC + oc
                        TE.wait(e, f"gate_{h}_{n}")
                        if n >= 2:
                            VE.wait(e, f"yg_{h}_{n - 2}")
                        e.activation(out=gat_ch[:, n % 2, :],
                                     in_=PS[n % 2][:, 0:TBh], func=AF.Silu)
                        SE.inc(e.drain())
                        SE.mark(f"gate_ev_{h}_{n}")

            @block.tensor
            def _(e, h=h, hi=hi):
                LdWt.wait(e, "wt_done_wo")
                VE.wait(e, f"yg_done_{h}")
                for ti, t in enumerate(a_tiles(h)):
                    tl = t - NTh * h
                    for ob in range(NOB):
                        k = ti * NOB + ob
                        bank = PS[k % 2]
                        if k >= 2:
                            VE.wait(e, f"wo_ev_{h}_{k - 2}")
                        elif hi >= 1:
                            VE.wait(e, f"wo_ev_{part2_halves[0]}_"
                                       f"{NTh * NOB - 2 + k}")
                        for fc in range(NC):
                            e.matmul(bank[:, 0:OBW],
                                     vT[:, fc, 128 * tl:128 * (tl + 1)],
                                     wt_wo[:, fc, OBW * ob:OBW * (ob + 1)],
                                     start=(fc == 0), stop=(fc == NC - 1))
                        TE.inc(e.drain())
                        TE.mark(f"wo_{h}_{k}")

            @block.vector
            def _(e, h=h, hi=hi):
                for ti, t in enumerate(a_tiles(h)):
                    gi = hi * NTh + ti
                    if gi >= 2:
                        tp = a_tiles(part2_halves[(gi - 2) // NTh])[(gi - 2) % NTh]
                        SSO[tp % 2].wait(e, f"ao_st_{tp}")
                    for ob in range(NOB):
                        k = ti * NOB + ob
                        bank = PS[k % 2]
                        TE.wait(e, f"wo_{h}_{k}")
                        e.tensor_copy(aostg[:, t % 2, OBW * ob:OBW * (ob + 1)],
                                      bank[:, 0:OBW])
                        VE.inc(e.drain())
                        VE.mark(f"wo_ev_{h}_{k}")
                VE.mark(f"wo_done_{h}")

            @block.sync
            def _(e, h=h):
                for ti, t in enumerate(a_tiles(h)):
                    VE.wait(e, f"wo_ev_{h}_{ti * NOB + NOB - 1}")
                    SSO[t % 2].inc(e.dma_start(
                        out=daout[128 * t:128 * (t + 1), :],
                        in_=aostg[:, t % 2, :]), 16)
                    SSO[t % 2].mark(f"ao_st_{t}")

        if cfg.get("upto") == "A":
            _cut_final()
            ovAC.close()
            return nc

        weight_transpose(dWqkv, OQKV, wqkv_t, "wqkv",
                         pre_wait=lambda e: VE.wait(e, "yg_done_0"))

        @block.tensor
        def _(e):
            LdWt.wait(e, "wt_done_wqkv")
            LdWs.wait(e, "smallw")
            VE.wait(e, "yg_done_0")
            for nb in range(OQKV // 512):
                if nb >= 1:
                    VE.wait(e, f"bq_ev_{nb - 1}")
                for fc in range(NC):
                    e.matmul(PS[4][0:1, 0:512], b1_all[:, fc:fc + 1],
                             wqkv_t[:, fc, 512 * nb:512 * (nb + 1)],
                             start=(fc == 0), stop=(fc == NC - 1))
                TE.inc(e.drain())
                TE.mark(f"bq_{nb}")

        @block.vector
        def _(e):
            for nb in range(OQKV // 512):
                TE.wait(e, f"bq_{nb}")
                e.tensor_copy(bq_row[:, 512 * nb:512 * (nb + 1)],
                              PS[4][0:1, 0:512])
                VE.inc(e.drain())
                VE.mark(f"bq_ev_{nb}")
            for fc in range(NC):
                e.tensor_scalar_mul(wqkv_t[:, fc, :], wqkv_t[:, fc, :],
                                    g1_all[:, fc:fc + 1])
            VE.inc(e.drain())
            VE.mark("wqkv_scaled")

        weight_transpose(dWpr, D, wt_wpr, "wpr",
                         pre_wait=lambda e: VE.wait(e, "wo_done_0"))

        pair_list = [(bq, bk) for bq in range(B) for bk in range(bq + 1)]
        pair_idx = {p: i for i, p in enumerate(pair_list)}


        for lh in range(2):
            @block.sync
            def _(e, lh=lh):
                SSX.wait(e, "xnst_1")
                if lh == 0:
                    TE.wait(e, f"gate_{0}_{NC * NTBh - 1}")
                else:
                    TE.wait(e, f"qkv_{0}_{NCq * NTBh - 1}")
                for b in range(B):
                    LdH.inc(e.dma_start(
                        out=xnT[:, :, Lh * b:Lh * (b + 1)],
                        in_=dxn[:, :, Lc * b + Lh * lh:
                                Lc * b + Lh * (lh + 1)].rearrange(
                                    "c p j -> p c j")), 16)
                LdH.mark(f"xnlh_{lh}")

            @block.tensor
            def _(e, lh=lh):
                SSR.wait(e, "rho_row")
                if lh == 0:
                    VE.wait(e, "rho_t16")
                    VE.wait(e, f"ystat_ev_0_{NTBh - 1}")
                else:
                    VE.wait(e, "attnall_0")
                for nb in range(Th // 512):
                    if nb >= 1:
                        VE.wait(e, f"rhobc_ev_{lh}_{nb - 1}")
                    elif lh >= 1:
                        VE.wait(e, f"rhobc_ev_{lh - 1}_{Th // 512 - 1}")
                    for sb2 in range(512 // Lh):
                        b = (512 * nb) // Lh + sb2
                        e.matmul(PS[5][:, Lh * sb2:Lh * (sb2 + 1)],
                                 ones_c1S[:],
                                 rho_row[:, Lc * b + Lh * lh:
                                         Lc * b + Lh * (lh + 1)],
                                 start=True, stop=True)
                    TE.inc(e.drain())
                    TE.mark(f"rhobc_{lh}_{nb}")

            @block.vector
            def _(e, lh=lh):
                for nb in range(Th // 512):
                    TE.wait(e, f"rhobc_{lh}_{nb}")
                    e.tensor_copy(rho_bc[:, 512 * nb:512 * (nb + 1)],
                                  PS[5][:, 0:512])
                    VE.inc(e.drain())
                    VE.mark(f"rhobc_ev_{lh}_{nb}")
                VE.mark(f"rhobc_done_{lh}")

            @block.tensor
            def _(e, lh=lh):
                VE.wait(e, "wqkv_scaled")
                LdH.wait(e, f"xnlh_{lh}")
                if lh == 0:
                    VE.wait(e, "wo_done_0")
                else:
                    VE.wait(e, f"prj_ev_0_{(NTh - 1) * NOB + NOB - 1}")
                for oc in range(NCq):
                    for tb in range(NTBh):
                        k = oc * NTBh + tb
                        bank = PS[k % 2]
                        if k >= 2:
                            VE.wait(e, f"qkv_ev_{lh}_{k - 2}")
                        for fc in range(NC):
                            e.matmul(bank[:, 0:TBh],
                                     wqkv_t[:, fc, 128 * oc:128 * (oc + 1)],
                                     xnT[:, fc, TBh * tb:TBh * (tb + 1)],
                                     start=(fc == 0), stop=False)
                        e.matmul(bank[:, 0:TBh],
                                 bq_row[:, 128 * oc:128 * (oc + 1)],
                                 ones_rowS[:, 0:TBh], start=False, stop=True)
                        TE.inc(e.drain())
                        TE.mark(f"qkv_{lh}_{k}")

            @block.vector
            def _(e, lh=lh):
                VE.wait(e, f"rhobc_done_{lh}")
                if lh == 0:
                    TE.wait(e, f"wo_{0}_{NTh * NOB - 1}")
                else:
                    VE.wait(e, "attn_done_0")
                    SSD.wait(e, "dup_dma_0")
                for oc in range(NCq):
                    g, r = oc // 3, oc % 3
                    for tb in range(NTBh):
                        k = oc * NTBh + tb
                        bank = PS[k % 2]
                        TE.wait(e, f"qkv_{lh}_{k}")
                        sl = slice(TBh * tb, TBh * (tb + 1))
                        if r < 2:
                            e.tensor_mul(vT[:, 2 * g + r, sl],
                                         bank[:, 0:TBh], rho_bc[:, sl])
                        else:
                            e.tensor_mul(kvX[0:64, g, sl],
                                         bank[0:64, 0:TBh], rho_bc[0:64, sl])
                            e.drain()
                            e.tensor_mul(kvX[64:128, NKV + g, sl],
                                         bank[64:128, 0:TBh],
                                         rho_bc[64:128, sl])
                        VE.inc(e.drain())
                        VE.mark(f"qkv_ev_{lh}_{k}")
                VE.mark(f"qkv_done_{lh}")

            @block.sync
            def _(e, lh=lh):
                VE.wait(e, f"qkv_done_{lh}")
                for g in range(NKV):
                    SSD.inc(e.dma_start(out=kvX[64:128, g, :],
                                        in_=kvX[0:64, g, :]), 16)
                    SSD.inc(e.dma_start(out=kvX[0:64, NKV + g, :],
                                        in_=kvX[64:128, NKV + g, :]), 16)
                SSD.mark(f"dup_dma_{lh}")

            def vdup(g):
                return kvX[:, NKV + g, :]

            def kdup(g):
                return kvX[:, g, :]

            @block.vector
            def _(e, lh=lh):
                SSD.wait(e, f"dup_dma_{lh}")
                for bq in range(B):
                    for bk in range(bq + 1):
                        i = pair_idx[(bq, bk)]
                        for qc in range(NC):
                            g = qc // 2
                            kk = i * NC + qc
                            if kk >= 2:
                                TE.wait(e, f"sc_mm_{lh}_{kk - 2}")
                            e.tensor_mul(
                                prod_scr[:, kk % 2, :],
                                vT[:, qc, Lh * bq:Lh * (bq + 1)],
                                kdup(g)[:, Lh * bk:Lh * (bk + 1)])
                            VE.inc(e.drain())
                            VE.mark(f"prod_{lh}_{kk}")
                    SE.wait(e, f"sc_mv_{lh}_{pair_idx[(bq, bq)]}")
                    e.tensor_copy(smx[:, :], PS[5][0:16, 0:Lh])
                    e.drain()
                    for bk in range(1, bq + 1):
                        e.tensor_max(smx[:, :], smx[:, :],
                                     PS[5][bk * 32:bk * 32 + 16, 0:Lh])
                        e.drain()
                    for bk in range(bq + 1):
                        e.tensor_sub(PS[5][bk * 32:bk * 32 + 16, 0:Lh],
                                     PS[5][bk * 32:bk * 32 + 16, 0:Lh],
                                     smx[:, :])
                    VE.inc(e.drain())
                    VE.mark(f"sm_sub_{lh}_{bq}")
                    SE.wait(e, f"sc_exp_{lh}_{bq}")
                    e.tensor_copy(srz[:, :], PS[5][0:16, 0:Lh])
                    e.drain()
                    for bk in range(1, bq + 1):
                        e.tensor_add(srz[:, :], srz[:, :],
                                     PS[5][bk * 32:bk * 32 + 16, 0:Lh])
                        e.drain()
                    e.reciprocal(srz[:, :], srz[:, :])
                    e.drain()
                    for bk in range(bq + 1):
                        i = pair_idx[(bq, bk)]
                        e.tensor_mul(attn_b[:, i, :],
                                     PS[5][bk * 32:bk * 32 + 16, 0:Lh],
                                     srz[:, :])
                    VE.inc(e.drain())
                    VE.mark(f"attn_{lh}_{bq}")
                VE.mark(f"attnall_{lh}")

            @block.tensor
            def _(e, lh=lh):
                if lh == 0:
                    VE.wait(e, f"bq_ev_{OQKV // 512 - 1}")
                for bq in range(B):
                    for bk in range(bq + 1):
                        i = pair_idx[(bq, bk)]
                        if i >= 1:
                            SE.wait(e, f"sc_mv_{lh}_{i - 1}")
                        for qc in range(NC):
                            kk = i * NC + qc
                            VE.wait(e, f"prod_{lh}_{kk}")
                            e.matmul(PS[4][0:16, 0:Lh],
                                     msegS[:, 16 * qc:16 * (qc + 1)],
                                     prod_scr[:, kk % 2, :],
                                     start=(qc == 0), stop=(qc == NC - 1))
                            TE.inc(e.drain())
                            TE.mark(f"sc_mm_{lh}_{kk}")

            @block.scalar
            def _(e, lh=lh):
                for bq in range(B):
                    for bk in range(bq + 1):
                        i = pair_idx[(bq, bk)]
                        TE.wait(e, f"sc_mm_{lh}_{i * NC + NC - 1}")
                        if bq >= 1 and bk < bq:
                            VE.wait(e, f"attn_{lh}_{bq - 1}")
                        elif lh >= 1 and bq == 0:
                            VE.wait(e, "attnall_0")
                        e.copy(out=PS[5][bk * 32:bk * 32 + 16, 0:Lh],
                               in_=PS[4][0:16, 0:Lh])
                        SE.inc(e.drain())
                        SE.mark(f"sc_mv_{lh}_{i}")
                    VE.wait(e, f"sm_sub_{lh}_{bq}")
                    for bk in range(bq + 1):
                        e.activation(out=PS[5][bk * 32:bk * 32 + 16, 0:Lh],
                                     in_=PS[5][bk * 32:bk * 32 + 16, 0:Lh],
                                     func=AF.Exp)
                    SE.inc(e.drain())
                    SE.mark(f"sc_exp_{lh}_{bq}")

            @block.tensor
            def _(e, lh=lh):
                kk = 0
                for bq in range(B):
                    VE.wait(e, f"attn_{lh}_{bq}")
                    if lh == 0 and bq == 0:
                        VE.wait(e, f"ybc_ev_0_{NTBh - 1}")
                    for bk in range(bq + 1):
                        i = pair_idx[(bq, bk)]
                        for qc in range(NC):
                            if kk >= 2:
                                VE.wait(e, f"av_ev_{lh}_{kk - 2}")
                            elif lh >= 1:
                                VE.wait(e, f"av_ev_{lh - 1}_"
                                           f"{NPAIR * NC - 2 + kk}")
                            e.matmul(PS[2 + (kk % 2)][:, 0:Lh],
                                     e16S[:, 128 * qc:128 * (qc + 1)],
                                     attn_b[:, i, :], start=True, stop=True)
                            TE.inc(e.drain())
                            TE.mark(f"av_{lh}_{kk}")
                            kk += 1

            @block.vector
            def _(e, lh=lh):
                kk = 0
                for bq in range(B):
                    for bk in range(bq + 1):
                        for qc in range(NC):
                            g = qc // 2
                            TE.wait(e, f"av_{lh}_{kk}")
                            bank = PS[2 + (kk % 2)]
                            if bk == 0:
                                e.tensor_mul(vT[:, qc, Lh * bq:Lh * (bq + 1)],
                                             bank[:, 0:Lh],
                                             vdup(g)[:, Lh * bk:Lh * (bk + 1)])
                            else:
                                e.tensor_mul(av_tmp[:, :], bank[:, 0:Lh],
                                             vdup(g)[:, Lh * bk:Lh * (bk + 1)])
                                e.drain()
                                e.tensor_add(vT[:, qc, Lh * bq:Lh * (bq + 1)],
                                             vT[:, qc, Lh * bq:Lh * (bq + 1)],
                                             av_tmp[:, :])
                            VE.inc(e.drain())
                            VE.mark(f"av_ev_{lh}_{kk}")
                            kk += 1
                VE.mark(f"attn_done_{lh}")

            def te_l2rtr(e, lh, gi, t):
                for fc in range(NC):
                    k2 = gi * NC + fc
                    if k2 >= 2:
                        SE.wait(e, f"l2T_ev_{k2 - 2}")
                    e.transpose(pta(k2),
                                ln2b[:, gi % 2, 128 * fc:128 * (fc + 1)],
                                ident_fS[:])
                    TE.inc(e.drain())
                    TE.mark(f"l2T_{k2}")
                SE.wait(e, f"l2T_evt_{gi}")
                for fc in range(NC):
                    e.matmul(PS[4][:, 0:8],
                             l2T[:, gi % 2, 128 * fc:128 * (fc + 1)],
                             wr_b[:, fc, :], start=(fc == 0),
                             stop=(fc == NC - 1))
                TE.inc(e.drain())
                TE.mark(f"rtr_{gi}")
                VE.wait(e, f"sgm_{gi}")

            @block.tensor
            def _(e, lh=lh):
                LdWt.wait(e, "wt_done_wpr")
                VE.wait(e, f"attn_done_{lh}")
                tl_ = c_tiles(lh)
                for ti, t in enumerate(tl_):
                    for ob in range(NOB):
                        k = ti * NOB + ob
                        bank = PS[k % 2]
                        if k >= 2:
                            VE.wait(e, f"prj_ev_{lh}_{k - 2}")
                        elif lh >= 1:
                            VE.wait(e, f"prj_ev_0_{NTh * NOB - 2 + k}")
                        for fc in range(NC):
                            e.matmul(bank[:, 0:OBW],
                                     vT[:, fc, lh_col(t, lh):
                                        lh_col(t, lh) + 128],
                                     wt_wpr[:, fc, OBW * ob:OBW * (ob + 1)],
                                     start=(fc == 0), stop=(fc == NC - 1))
                        TE.inc(e.drain())
                        TE.mark(f"prj_{lh}_{k}")
                    if ti >= 1:
                        VE.wait(e, f"ln2_{lh * NTh + ti - 1}")
                        te_l2rtr(e, lh, lh * NTh + ti - 1, tl_[ti - 1])
                VE.wait(e, f"ln2_{lh * NTh + NTh - 1}")
                te_l2rtr(e, lh, lh * NTh + NTh - 1, tl_[NTh - 1])

            @block.sync
            def _(e, lh=lh):
                if lh == 0:
                    SSO[0].wait(e, f"ao_st_{a_tiles(0)[-2]}")
                    SSO[1].wait(e, f"ao_st_{a_tiles(0)[-1]}")
                for ti, t in enumerate(c_tiles(lh)):
                    gi = lh * NTh + ti
                    if gi >= 2:
                        VE.wait(e, f"acc_{gi - 2}")
                        SSA[gi % 2].wait(e, f"acc_st_{ctile_order[gi - 2]}")
                    LX = (LdX0, LdX1)[gi % 2]
                    LX.inc(e.dma_start(out=xbuf[:, gi % 2, :],
                                       in_=dx[128 * t:128 * (t + 1), :]), 16)
                    LX.mark(f"x2_{t}")
                    LA = (LdA0, LdA1)[gi % 2]
                    LA.inc(e.dma_start(out=aostg[:, gi % 2, :],
                                       in_=daout[128 * t:128 * (t + 1), :]), 16)
                    LA.mark(f"ao2_{t}")
                    VE.wait(e, f"acc_{gi}")
                    SSA[gi % 2].inc(e.dma_start(
                        out=dout[128 * t:128 * (t + 1), :],
                        in_=xbuf[:, gi % 2, :]), 16)
                    SSA[gi % 2].mark(f"acc_st_{t}")
                    VE.wait(e, f"ln2_{gi}")
                    SSL[gi % 2].inc(e.dma_start(
                        out=dln2[128 * t:128 * (t + 1), :],
                        in_=ln2b[:, gi % 2, :]), 16)
                    SSL[gi % 2].mark(f"ln2_st_{t}")

            @block.vector
            def _(e, lh=lh):
                for ti, t in enumerate(c_tiles(lh)):
                    gi = lh * NTh + ti
                    (LdX0, LdX1)[gi % 2].wait(e, f"x2_{t}")
                    (LdA0, LdA1)[gi % 2].wait(e, f"ao2_{t}")
                    if gi >= 2:
                        tp = ctile_order[gi - 2]
                        SSA[gi % 2].wait(e, f"acc_st_{tp}")
                        SSL[gi % 2].wait(e, f"ln2_st_{tp}")
                    e.tensor_scalar(xn_scr[:], xbuf[:, gi % 2, :],
                                    st_r[:, t:t + 1], st_nmur[:, t:t + 1],
                                    ALU.mult, ALU.add)
                    e.drain()
                    for ob in range(NOB):
                        k = ti * NOB + ob
                        TE.wait(e, f"prj_{lh}_{k}")
                        sl = slice(OBW * ob, OBW * (ob + 1))
                        e.tensor_add(h1b[:, gi % 2, sl], xn_scr[:, sl],
                                     PS[k % 2][:, 0:OBW])
                        VE.inc(e.drain())
                        VE.mark(f"prj_ev_{lh}_{k}")
                    e.tensor_add(xbuf[:, gi % 2, :], xbuf[:, gi % 2, :],
                                 aostg[:, gi % 2, :])
                    e.drain()
                    e.tensor_add(xbuf[:, gi % 2, :], xbuf[:, gi % 2, :],
                                 h1b[:, gi % 2, :])
                    VE.inc(e.drain())
                    VE.mark(f"acc_{gi}")
                    e.reduce_sum(st2_sum[:, t:t + 1], h1b[:, gi % 2, :],
                                 axis=AX.X)
                    SE.wait(e, f"sq2_{gi}")
                    e.drain()
                    e.tensor_scalar_mul(st_mu[:, t:t + 1], st2_sum[:, t:t + 1],
                                        1.0 / D)
                    e.drain()
                    e.scalar_tensor_tensor(st_tmp2[:, t:t + 1],
                                           st_mu[:, t:t + 1], -1.0,
                                           st_mu[:, t:t + 1],
                                           ALU.mult, ALU.mult)
                    e.tensor_scalar_mul(st_var[:, t:t + 1], st2_sq[:, t:t + 1],
                                        1.0 / D)
                    e.drain()
                    e.tensor_add(st_var[:, t:t + 1], st_var[:, t:t + 1],
                                 st_tmp2[:, t:t + 1])
                    VE.inc(e.drain())
                    VE.mark(f"var2_{gi}")
                    SE.wait(e, f"sqrt2_{gi}")
                    e.reciprocal(st2_r[:, t:t + 1], st_tmp[:, t:t + 1])
                    e.drain()
                    e.scalar_tensor_tensor(st2_nmur[:, t:t + 1],
                                           st_mu[:, t:t + 1], -1.0,
                                           st2_r[:, t:t + 1],
                                           ALU.mult, ALU.mult)
                    e.drain()
                    e.tensor_scalar(ln2b[:, gi % 2, :], h1b[:, gi % 2, :],
                                    st2_r[:, t:t + 1], st2_nmur[:, t:t + 1],
                                    ALU.mult, ALU.add)
                    e.drain()
                    e.tensor_mul(ln2b[:, gi % 2, :], ln2b[:, gi % 2, :],
                                 g2bc[:, :])
                    e.drain()
                    e.tensor_add(ln2b[:, gi % 2, :], ln2b[:, gi % 2, :],
                                 b2bc[:, :])
                    VE.inc(e.drain())
                    VE.mark(f"ln2_{gi}")
                    TE.wait(e, f"rtr_{gi}")
                    e.tensor_copy(logit_s[:], PS[4][:, 0:8])
                    e.drain()
                    e.max(out=mx8[:], in_=logit_s[:])
                    e.drain()
                    e.max_index(ix8[:], mx8[:], logit_s[:])
                    e.drain()
                    e.tensor_copy(routepack[:, 0, t:t + 1], ix8[:, 0:1])
                    e.tensor_copy(routepack[:, 1, t:t + 1], ix8[:, 1:2])
                    e.tensor_sub(sgm[:], mx8[:, 1:2], mx8[:, 0:1])
                    VE.inc(e.drain())
                    VE.mark(f"sgm_{gi}")
                    SE.wait(e, f"sig_{gi}")
                    e.tensor_scalar(routepack[:, 2, t:t + 1],
                                    routepack[:, 3, t:t + 1], -1.0, 1.0,
                                    ALU.mult, ALU.add)
                    VE.inc(e.drain())
                    VE.mark(f"route_{gi}")

            @block.scalar
            def _(e, lh=lh):
                for ti, t in enumerate(c_tiles(lh)):
                    gi = lh * NTh + ti
                    VE.wait(e, f"prj_ev_{lh}_{ti * NOB + NOB - 1}")
                    e.activation(out=sq_scr[:], in_=h1b[:, gi % 2, :],
                                 func=AF.Square, accum_out=st2_sq[:, t:t + 1])
                    SE.inc(e.drain())
                    SE.mark(f"sq2_{gi}")
                    VE.wait(e, f"var2_{gi}")
                    e.activation(out=st_tmp[:, t:t + 1], in_=st_var[:, t:t + 1],
                                 func=AF.Sqrt, bias=eps128[:])
                    SE.inc(e.drain())
                    SE.mark(f"sqrt2_{gi}")
                    for fc in range(NC):
                        k = gi * NC + fc
                        TE.wait(e, f"l2T_{k}")
                        e.copy(out=l2T[:, gi % 2, 128 * fc:128 * (fc + 1)],
                               in_=pta(k))
                        SE.inc(e.drain())
                        SE.mark(f"l2T_ev_{k}")
                    SE.mark(f"l2T_evt_{gi}")
                    VE.wait(e, f"sgm_{gi}")
                    e.activation(out=routepack[:, 3, t:t + 1], in_=sgm[:],
                                 func=AF.Sigmoid)
                    SE.inc(e.drain())
                    SE.mark(f"sig_{gi}")


        if cfg.get("upto") == "C":
            _cut_final()
            ovAC.close()
            return nc

        @block.tensor
        def _(e):
            VE.wait(e, f"route_{NT - 1}")
            for d in range(8):
                if d >= 1:
                    VE.wait(e, f"fold_ev_{d - 1}")
                e.matmul(PS[5][0:16, 0:4 * NT], selS[:, 16 * d:16 * (d + 1)],
                         routepack.rearrange("p a t -> p (a t)"),
                         start=True, stop=True)
                TE.inc(e.drain())
                TE.mark(f"fold_{d}")

        @block.vector
        def _(e):
            H = NAC // 2
            for d in range(8):
                TE.wait(e, f"fold_{d}")
                e.tensor_copy(
                    E_IL[:, 0:H].rearrange("q (j s) -> q j s", s=8)[:, :, d],
                    PS[5][0:16, 0 * NT:1 * NT])
                e.tensor_copy(
                    E_IL[:, H:].rearrange("q (j s) -> q j s", s=8)[:, :, d],
                    PS[5][0:16, 1 * NT:2 * NT])
                e.tensor_copy(
                    W_IL[:, 0:H].rearrange("q (j s) -> q j s", s=8)[:, :, d],
                    PS[5][0:16, 2 * NT:3 * NT])
                e.tensor_copy(
                    W_IL[:, H:].rearrange("q (j s) -> q j s", s=8)[:, :, d],
                    PS[5][0:16, 3 * NT:4 * NT])
                VE.inc(e.drain())
                VE.mark(f"fold_ev_{d}")
            VE.mark("fold_done")

        @block.sync
        def _(e):
            VE.wait(e, "fold_done")
            SSW.inc(e.dma_start(out=dwslot, in_=W_IL[:]), 16)
            SSW.mark("wslot")

        @block.vector
        def _(e):
            for ex in range(E):
                if ex >= 1:
                    GP.wait(e, f"sg_{ex - 1}")
                e.tensor_single_scalar(candf[:], E_IL[:], float(ex),
                                       ALU.is_equal)
                e.drain()
                e.tensor_mul(candf[:], candf[:], aidp1S[:])
                e.drain()
                e.tensor_scalar_add(candf[:], candf[:], -1.0)
                e.drain()
                e.tensor_copy(candi[:], candf[:])
                VE.inc(e.drain())
                VE.mark(f"cand_{ex}")
                GP.wait(e, f"sg_{ex}")
                e.tensor_copy(aidx[0:16, ex, :], sgdst[:, 0:NIDX])
                VE.inc(e.drain())
                VE.mark(f"aidx_{ex}")

        @block.gpsimd
        def _(e):
            VE.wait(e, "fold_done")
            e.load_library(LIBS["sparse_gather"])
            for ex in range(E):
                VE.wait(e, f"cand_{ex}")
                if ex >= 1:
                    VE.wait(e, f"aidx_{ex - 1}")
                GP.inc(e.sparse_gather(sgdst[:], candi[:],
                                       num_found=nf_t[:, ex:ex + 1]), 1)
                GP.mark(f"sg_{ex}")

        @block.vector
        def _(e):
            GP.wait(e, f"sg_{E - 1}")
            e.tensor_copy(nf_f[:], nf_t[:])
            VE.inc(e.drain())
            VE.mark("nf_f")
            TE.wait(e, "nfbc")
            e.tensor_copy(nf16[:], PS[5][0:16, 0:E])
            e.drain()
            for ex in range(E):
                lg = NAC.bit_length() - 1
                assert (1 << lg) == NAC
                e.tensor_single_scalar(st3[:, 2 * NIDX:3 * NIDX], o16S[:],
                                       nf16[:, ex:ex + 1], ALU.subtract)
                e.tensor_copy(tk_a[:], aidx[0:16, ex, :])
                e.drain()
                e.tensor_single_scalar(st3[:, 2 * NIDX:3 * NIDX],
                                       st3[:, 2 * NIDX:3 * NIDX], 0.0,
                                       ALU.is_lt)
                e.tensor_single_scalar(tk_b[:], tk_a[:], NAC - 1,
                                       ALU.bitwise_and)
                e.tensor_single_scalar(tk_c[:], tk_a[:], lg,
                                       ALU.arith_shift_right)
                e.drain()
                e.tensor_single_scalar(tk_b[:], tk_b[:], NAC // 2 - 1,
                                       ALU.bitwise_and)
                e.drain()
                e.tensor_single_scalar(tk_b[:], tk_b[:], 4,
                                       ALU.logical_shift_left)
                e.drain()
                e.tensor_add(tk_b[:], tk_b[:], tk_c[:])
                e.drain()
                e.tensor_copy(st3[:, NIDX:2 * NIDX], tk_b[:])
                e.tensor_single_scalar(tk_c[:], tk_a[:], 0, ALU.max)
                e.drain()
                e.tensor_single_scalar(tk_c[:], tk_c[:], NA - 1, ALU.min)
                e.drain()
                e.tensor_copy(st3[:, 0:NIDX], tk_c[:])
                VE.inc(e.drain())
                VE.mark(f"tok_{ex}")
                TE.wait(e, f"unf_{ex}")
                e.tensor_copy(idxu[:, ex, :], PS[4][:, 0:CAPT])
                e.tensor_copy(uv[:], PS[4][:, 2 * CAPT:3 * CAPT])
                e.drain()
                e.tensor_mul(ug[:], PS[4][:, CAPT:2 * CAPT], uv[:])
                e.drain()
                e.tensor_copy(gidx[:, ex, :], ug[:])
                e.tensor_scalar(uv[:], uv[:], -1.0, 1.0, ALU.mult, ALU.add)
                e.drain()
                e.tensor_mul(uv[:], uv[:], tpS[:])
                e.drain()
                e.tensor_add(ug[:], ug[:], uv[:])
                e.drain()
                e.tensor_copy(sidx[:, ex, :], ug[:])
                VE.inc(e.drain())
                VE.mark(f"idxu_{ex}")
            VE.mark("rt_done")

        @block.tensor
        def _(e):
            VE.wait(e, "fold_done")
            VE.wait(e, "nf_f")
            e.matmul(PS[5][0:16, 0:E], ones_c1S[:, 0:16], nf_f[:],
                     start=True, stop=True)
            TE.inc(e.drain())
            TE.mark("nfbc")
            for ex in range(E):
                VE.wait(e, f"tok_{ex}")
                if ex >= 1:
                    VE.wait(e, f"idxu_{ex - 1}")
                for d in range(8):
                    e.matmul(PS[4][:, 0:3 * CAPT],
                             seltS[:, 128 * d:128 * (d + 1)],
                             st3[:, 0:3 * NIDX].rearrange(
                                 "q (c s) -> q c s", s=8)[:, :, d],
                             start=(d == 0), stop=(d == 7))
                TE.inc(e.drain())
                TE.mark(f"unf_{ex}")

        if cfg.get("upto") == "RT":
            _cut_final()
            ovAC.close()
            return nc

        ovAC.close()
        sbD = mksb(ctx)
        gtok = sbD("gtok", [128, CAPT, D])
        gT = sbD("gT", [128, NC, CAP], BF16)
        fc1T = sbD("fc1T", [128, NF, CAP], BF16)
        fc2w = sbD("fc2w", [128, CAPT, D])
        gw = sbD("gw", [128, CAPT, 1])
        w1sb = [sbD("w1sb0", [128, NC, FF], BF16),
                sbD("w1sb1", [128, NC, FF], BF16)]
        w2sb = sbD("w2sb", [128, NF, D], BF16)

        last_par_tile = {}
        for gi, t in enumerate(ctile_order):
            last_par_tile[gi % 2] = t

        @block.gpsimd
        def _(e):
            GP.wait(e, f"sg_{E - 1}")
            VE.wait(e, "rt_done")
            for par in range(2):
                SSA[par].wait(e, f"acc_st_{last_par_tile[par]}")
                SSL[par].wait(e, f"ln2_st_{last_par_tile[par]}")
            for ex in range(E):
                if ex == 0:
                    LdW1.inc(e.dma_start(
                        out=w1sb[0][:],
                        in_=dW1[0].rearrange("(c p) f -> p c f", p=128)), 16)
                    LdW1.mark("w1_0")
                if ex >= 1:
                    TE.wait(e, f"gT_done_{ex - 1}")
                for ct in range(CAPT):
                    SGT.inc(e.indirect_dma_start(
                        out=gtok[:, ct, :], out_offset=None, in_=dln2,
                        in_offset=IndirectOffsetOnAxis(
                            ap=gidx[:, ex, ct:ct + 1], axis=0)), 16)
                SGT.mark(f"gath_{ex}")
                SSW.wait(e, "wslot")
                if ex >= 1:
                    SGW.wait(e, f"gathw_{ex - 1}")
                for ct in range(CAPT):
                    SGW.inc(e.indirect_dma_start(
                        out=gw[:, ct, :], out_offset=None,
                        in_=dwslot.rearrange("q c -> (q c)")[:, None],
                        in_offset=IndirectOffsetOnAxis(
                            ap=idxu[:, ex, ct:ct + 1], axis=0)), 16)
                SGW.mark(f"gathw_{ex}")
                if ex + 1 < E:
                    if ex >= 1:
                        SE.wait(e, f"fc1_done_{ex - 1}")
                    LdW1.wait(e, f"w1_{ex}")
                    LdW1.inc(e.dma_start(
                        out=w1sb[(ex + 1) % 2][:],
                        in_=dW1[ex + 1].rearrange("(c p) f -> p c f", p=128)),
                        16)
                    LdW1.mark(f"w1_{ex + 1}")
                if ex >= 1:
                    VE.wait(e, f"fc2_done_{ex - 1}")
                    LdW2.wait(e, f"w2_{ex - 1}")
                LdW2.inc(e.dma_start(
                    out=w2sb[:],
                    in_=dW2[ex].rearrange("(c p) f -> p c f", p=128)), 16)
                LdW2.mark(f"w2_{ex}")
                VE.wait(e, f"fc2w_{ex}")
                for ct in range(CAPT):
                    SGS.inc(e.indirect_dma_start(
                        out=dout, out_offset=IndirectOffsetOnAxis(
                            ap=sidx[:, ex, ct:ct + 1], axis=0),
                        in_=fc2w[:, ct, :], in_offset=None,
                        compute_op=ALU.add), 16)
                SGS.mark(f"scat_{ex}")
                e.wait_ge(SGS.h, SGS.n)
            e.wait_ge(SGS.h, SGS.n)
            e.wait_ge(SGT.h, SGT.n)
            e.wait_ge(SGW.h, SGW.n)
            e.wait_ge(SSA[0].h, SSA[0].n)
            e.wait_ge(SSA[1].h, SSA[1].n)

        @block.tensor
        def _(e):
            for ex in range(E):
                SGT.wait(e, f"gath_{ex}")
                k = 0
                for ct in range(CAPT):
                    for fc in range(NC):
                        if k >= 2:
                            SE.wait(e, f"gT_ev_{ex}_{k - 2}")
                        elif ex >= 1:
                            SE.wait(e, f"gT_ev_{ex - 1}_"
                                       f"{CAPT * NC - 2 + k}")
                        e.transpose(pta(k),
                                    gtok[:, ct, 128 * fc:128 * (fc + 1)],
                                    ident_fS[:])
                        TE.inc(e.drain())
                        TE.mark(f"gT_{ex}_{k}")
                        k += 1
                SE.wait(e, f"gT_evdone_{ex}")
                TE.inc(e.drain())
                TE.mark(f"gT_done_{ex}")
                LdW1.wait(e, f"w1_{ex}")
                for Fc in range(NF):
                    if Fc >= 1:
                        SE.wait(e, f"fc1_ev_{ex}_{Fc - 1}")
                    for fc in range(NC):
                        for bi, (o0, w) in enumerate(cap_blocks):
                            e.matmul(PS[bi][:, 0:w],
                                     w1sb[ex % 2][:, fc, 128 * Fc:128 * (Fc + 1)],
                                     gT[:, fc, o0:o0 + w],
                                     start=(fc == 0), stop=(fc == NC - 1))
                    TE.inc(e.drain())
                    TE.mark(f"fc1_{ex}_{Fc}")
                LdW2.wait(e, f"w2_{ex}")
                SE.wait(e, f"fc1_done_{ex}")
                for ct in range(CAPT):
                    for ob in range(NOB):
                        kk = ct * NOB + ob
                        bank = PS[kk % 2]
                        if kk >= 2:
                            VE.wait(e, f"fc2_ev_{ex}_{kk - 2}")
                        for Fc in range(NF):
                            e.matmul(bank[:, 0:OBW],
                                     fc1T[:, Fc, 128 * ct:128 * (ct + 1)],
                                     w2sb[:, Fc, OBW * ob:OBW * (ob + 1)],
                                     start=(Fc == 0), stop=(Fc == NF - 1))
                        TE.inc(e.drain())
                        TE.mark(f"fc2_{ex}_{kk}")

        @block.scalar
        def _(e):
            for ex in range(E):
                k = 0
                for ct in range(CAPT):
                    for fc in range(NC):
                        TE.wait(e, f"gT_{ex}_{k}")
                        e.copy(out=gT[:, fc, 128 * ct:128 * (ct + 1)],
                               in_=pta(k))
                        SE.inc(e.drain())
                        SE.mark(f"gT_ev_{ex}_{k}")
                        k += 1
                SE.mark(f"gT_evdone_{ex}")
                for Fc in range(NF):
                    TE.wait(e, f"fc1_{ex}_{Fc}")
                    for bi, (o0, w) in enumerate(cap_blocks):
                        e.activation(out=fc1T[:, Fc, o0:o0 + w],
                                     in_=PS[bi][:, 0:w],
                                     func=AF.Gelu_apprx_tanh)
                    SE.inc(e.drain())
                    SE.mark(f"fc1_ev_{ex}_{Fc}")
                SE.mark(f"fc1_done_{ex}")

        @block.vector
        def _(e):
            for ex in range(E):
                SGW.wait(e, f"gathw_{ex}")
                if ex >= 1:
                    SGS.wait(e, f"scat_{ex - 1}")
                for ct in range(CAPT):
                    for ob in range(NOB):
                        kk = ct * NOB + ob
                        bank = PS[kk % 2]
                        TE.wait(e, f"fc2_{ex}_{kk}")
                        e.tensor_scalar_mul(
                            fc2w[:, ct, OBW * ob:OBW * (ob + 1)],
                            bank[:, 0:OBW], gw[:, ct, :])
                        VE.inc(e.drain())
                        VE.mark(f"fc2_ev_{ex}_{kk}")
                VE.inc(e.drain())
                VE.mark(f"fc2w_{ex}")
                VE.mark(f"fc2_done_{ex}")

    return nc



def _in_maps(cfg, inputs):
    n = cfg["n_cores"]
    x = np.ascontiguousarray(inputs["L0_attn_x"], dtype=np.float32)
    B, L, D = x.shape
    Lc = L // n

    def f32(a):
        return np.ascontiguousarray(a, dtype=np.float32)

    maps = []
    for c in range(n):
        shard = np.ascontiguousarray(x[:, Lc * c:Lc * (c + 1), :]).reshape(-1, D)
        maps.append(dict(
            x=shard,
            WqT=f32(inputs["L0_attn_Wq"].T), WvT=f32(inputs["L0_attn_Wv"].T),
            WoT=f32(inputs["L0_attn_Wo"].T), WgT=f32(inputs["L0_attn_Wg"].T),
            decay=f32(inputs["L0_attn_decay"]),
            ln_g=f32(inputs["L0_attn_ln_g"]), ln_b=f32(inputs["L0_attn_ln_b"]),
            ffn_g1=f32(inputs["L0_ffn_ln1_weight"]),
            ffn_b1=f32(inputs["L0_ffn_ln1_bias"]),
            ffn_g2=f32(inputs["L0_ffn_ln2_weight"]),
            ffn_b2=f32(inputs["L0_ffn_ln2_bias"]),
            WqkvT=f32(inputs["L0_ffn_qkv_weight"].T),
            WprojT=f32(inputs["L0_ffn_proj_weight"].T),
            Wr=f32(inputs["L0_ffn_router_weight"]),
            moe_w1=f32(inputs["L0_ffn_moe_w1"]),
            moe_w2=f32(inputs["L0_ffn_moe_w2"]),
        ))
    return maps


def kernel(**inputs):
    cfg = make_cfg()
    n = cfg["n_cores"]
    tags = {}
    build_nc(cfg, tags, final=False)
    nc = build_nc(cfg, tags, final=True)
    mybir.codegen_inst_isa_subclasses(nc)

    maps = _in_maps(cfg, inputs)
    res = run_bass_kernel_spmd(nc, maps, core_ids=list(range(n)))

    B, L, D = inputs["L0_attn_x"].shape
    Lc = L // n
    out = np.empty((B, L, D), np.float32)
    for c in range(n):
        out[:, Lc * c:Lc * (c + 1), :] = \
            res.results[c]["out"][:B * Lc].reshape(B, Lc, D)
    return out
